# version 24
# baseline (speedup 1.0000x reference)
import numpy as np
import ml_dtypes
import concourse.bass as bass
import concourse.mybir as mybir
from concourse.bass_utils import run_bass_kernel_spmd

F32 = mybir.dt.float32
BF16 = mybir.dt.bfloat16
AF = mybir.ActivationFunctionType
ALU = mybir.AluOpType
AX = mybir.AxisListType

D = 1024
H = 16
HD = 64
NK = 4096
NQ = 2048
F = 4096
NE = 10
EPS = 1e-5


class Buf:
    __slots__ = ("w", "rs")

    def __init__(self):
        self.w = None
        self.rs = []


class Prog:
    ENGINES = ("tensor", "vector", "scalar", "gpsimd", "sync")

    def __init__(self):
        self.q = {e: [] for e in self.ENGINES}
        self.cnt = {}
        self.step = {}
        self.hwm = {}

    def sem(self, name, step):
        self.cnt[name] = 0
        self.step[name] = step

    def _wait(self, eng, tok):
        s, v = tok
        if v <= 0 or v <= self.hwm.get((eng, s), 0):
            return
        self.hwm[(eng, s)] = v
        self.q[eng].append(("w", s, v))

    def do(self, eng, fn, reads=(), writes=(), sem=None, after=()):
        toks = []
        for b in reads:
            if b.w:
                toks.append(b.w)
        for b in writes:
            if b.w:
                toks.append(b.w)
            toks.extend(b.rs)
        toks.extend(after)
        for t in sorted(set(toks)):
            self._wait(eng, t)
        s = sem if sem is not None else eng[0]
        self.cnt[s] += self.step[s]
        tok = (s, self.cnt[s])
        self.q[eng].append(("op", fn, s, self.step[s]))
        for b in writes:
            b.w = tok
            b.rs = []
        for b in reads:
            b.rs.append(tok)
        return tok

    def group(self, eng, fns, reads=(), writes=(), sem=None, after=()):
        toks = []
        for b in reads:
            if b.w:
                toks.append(b.w)
        for b in writes:
            if b.w:
                toks.append(b.w)
            toks.extend(b.rs)
        toks.extend(after)
        for t in sorted(set(toks)):
            self._wait(eng, t)
        s = sem if sem is not None else eng[0]
        for fn in fns[:-1]:
            self.q[eng].append(("op", fn, None, 0))
        self.cnt[s] += self.step[s]
        tok = (s, self.cnt[s])
        self.q[eng].append(("op", fns[-1], s, self.step[s]))
        for b in writes:
            b.w = tok
            b.rs = []
        for b in reads:
            b.rs.append(tok)
        return tok

    def fence(self):
        for e in self.ENGINES:
            for s, v in self.cnt.items():
                self._wait(e, (s, v))

    def finalize(self, nc, handles):
        with nc.Block() as blk:
            for ename in self.ENGINES:
                items = self.q[ename]
                if not items:
                    continue

                def run(eng, items=items):
                    for it in items:
                        if it[0] == "w":
                            eng.wait_ge(handles[it[1]], it[2])
                        else:
                            inst = it[1](eng)
                            if it[2] is not None:
                                inst.then_inc(handles[it[2]], it[3])
                getattr(blk, ename)(run)


def build_nc(dbg=None):
    nc = bass.Bass("TRN2", target_bir_lowering=False, debug=False)

    def din(name, shape, dt=F32):
        return nc.dram_tensor(name, shape, dt, kind="ExternalInput").ap()

    xq_h = din("xq", [NQ, D])
    xkv_h = din("xkv", [NK, D])
    wqb_h = din("wqb", [8, 8, 128, 128])
    wkb_h = din("wkb", [8, 8, 128, 128])
    wvT_h = din("wvT", [D, D])
    centT_h = din("centT", [D, 8])
    rb8_h = din("rb8", [128, 8])
    io8_h = din("io8", [128, 8])
    rtm_h = din("rtm", [128, 128])
    idf_h = din("idf", [128, 128])
    idbf_h = din("idbf", [128, 128], BF16)
    cosq_h = din("cosq", [128, NQ])
    sinq_h = din("sinq", [128, NQ])
    cosk_h = din("cosk", [128, NK])
    sink_h = din("sink", [128, NK])
    wfcT_h = din("wfcT", [NE, 32, 8, 128, 128])
    wprojT_h = din("wprojT", [NE, 8, 32, 128, 128])

    out_h = nc.dram_tensor("out", [NQ, D], F32, kind="ExternalOutput").ap()

    attnZ_h = nc.dram_tensor("attnZ", [NQ, D], F32, kind="Internal").ap()
    yscr_h = nc.dram_tensor("yscr", [NQ, D], F32, kind="Internal").ap()
    lnf_h = nc.dram_tensor("lnf", [16, 128, D], F32, kind="Internal").ap()

    arena = nc.alloc_sbuf_tensor("arena", [128, 41472], F32)

    def AV(off4, n_elems, dt):
        if dt == F32:
            return arena[:, off4:off4 + n_elems]
        return arena[:, off4:off4 + n_elems // 2].bitcast(dt)

    K_f = AV(0, 32768, BF16)
    Q_f = AV(16384, 16384, BF16)
    V_au = AV(24576, 33792, BF16)
    cosb = AV(24576, 4096, F32)
    sinb = AV(28672, 4096, F32)
    acc = AV(0, 8192, F32)
    ln3f = AV(8192, 16384, BF16)
    h_f = AV(16384, 32768, BF16)
    wfcb = [AV(32768, 1024, BF16), AV(33280, 1024, BF16),
            AV(39936, 1024, BF16), AV(40448, 1024, BF16)]
    wpb = [AV(33792, 4096, BF16), AV(35840, 4096, BF16)]
    cbc = AV(37888, 1024, F32)
    tmpw = AV(38912, 1024, F32)
    combrow_w = tmpw[0:1, 0:1024]
    qrawB = AV(39936, 512, F32)
    rt1B = AV(40448, 512, F32)
    wbA = [AV(32768, 1024, BF16), AV(33280, 1024, BF16),
           AV(33792, 1024, BF16), AV(34304, 1024, BF16)]
    lnb1 = AV(40960, 1024, BF16)
    azt1 = AV(16384, 1024, F32)
    xqt1 = AV(17408, 1024, F32)
    yt1 = AV(18432, 1024, F32)
    tmpf1 = AV(19456, 1024, F32)

    def sb(name, shape, dt=F32):
        return nc.alloc_sbuf_tensor(name, shape, dt)

    xt = sb("xt", [128, 4096])
    pb0 = xt[:, 0:512].bitcast(BF16)
    pb1 = xt[:, 512:1024].bitcast(BF16)
    Qz = [xt[:, 1024:1536].bitcast(BF16), xt[:, 1536:2048].bitcast(BF16)]
    atf = xt[:, 2048:4096]
    azt = xt[:, 0:1024]
    xqt = xt[:, 1024:2048]
    yt = xt[:, 2048:3072]
    tmpf = xt[:, 3072:4096]
    ytile = xt[:, 0:1024]
    ot = xt[:, 1024:2048]
    yt2 = xt[:, 2048:3072]

    lnT = sb("lnT", [128, 4096], BF16)
    lnb = sb("lnb", [128, 1024], BF16)
    wb0 = sb("wb0", [128, 1024], BF16)
    wb1 = sb("wb1", [128, 1024], BF16)
    qraw = sb("qraw", [128, 512])
    rt1 = sb("rt1", [128, 512])
    rt2 = sb("rt2", [128, 512])
    rtm = sb("rtm_t", [128, 128])
    idf = sb("idf_t", [128, 128])
    idbf = sb("idbf_t", [128, 128], BF16)
    centT = sb("centT_t", [128, 64])
    rb8 = sb("rb8_t", [128, 8])
    io8 = sb("io8_t", [128, 8])
    ones1 = sb("ones1", [1, 128])
    comb = sb("comb", [128, 128])
    junk = sb("junk", [128, 1024], BF16)
    az0 = sb("az0", [128, 64])
    az1 = sb("az1", [128, 64])
    rl = sb("rl", [128, 1])
    scol = [sb("scol0", [128, 1]), sb("scol1", [128, 1])]
    s2col = [sb("s2col0", [128, 1]), sb("s2col1", [128, 1])]
    mcol = [sb("mcol0", [128, 1]), sb("mcol1", [128, 1])]
    varc = [sb("varc0", [128, 1]), sb("varc1", [128, 1])]
    mm2 = [sb("mm20", [128, 1]), sb("mm21", [128, 1])]
    rstdc = [sb("rstdc0", [128, 1]), sb("rstdc1", [128, 1])]
    negmr = [sb("negmr0", [128, 1]), sb("negmr1", [128, 1])]
    r8 = [sb("r8_0", [8, 128]), sb("r8_1", [8, 128])]
    raw8 = [sb("raw8_0", [128, 8]), sb("raw8_1", [128, 8])]
    sc8 = [sb("sc8_0", [128, 8]), sb("sc8_1", [128, 8])]
    bal8 = [sb("bal8_0", [128, 8]), sb("bal8_1", [128, 8])]
    bal8b = [sb("bal8b_0", [128, 8]), sb("bal8b_1", [128, 8])]
    bal2 = [sb("bal2_0", [128, 8]), sb("bal2_1", [128, 8])]
    eq8 = [sb("eq8_0", [128, 8]), sb("eq8_1", [128, 8])]
    mi8 = [sb("mi8_0", [128, 8]), sb("mi8_1", [128, 8])]
    oh1 = [sb("oh1_0", [128, 8]), sb("oh1_1", [128, 8])]
    oh2 = [sb("oh2_0", [128, 8]), sb("oh2_1", [128, 8])]
    tmp8 = [sb("tmp8_0", [128, 8]), sb("tmp8_1", [128, 8])]
    m1c = [sb("m1c_0", [128, 1]), sb("m1c_1", [128, 1])]
    i1c = [sb("i1c_0", [128, 1]), sb("i1c_1", [128, 1])]
    g1c = [sb("g1c_0", [128, 1]), sb("g1c_1", [128, 1])]
    g2c = [sb("g2c_0", [128, 1]), sb("g2c_1", [128, 1])]
    d12 = [sb("d12_0", [128, 1]), sb("d12_1", [128, 1])]
    w1c = [sb("w1c_0", [128, 1]), sb("w1c_1", [128, 1])]
    w2c = [sb("w2c_0", [128, 1]), sb("w2c_1", [128, 1])]

    lnbs = [lnb, lnb1]

    pars = nc.alloc_psum_tensor("pars", [128, 4096], F32)
    pq = [pars[:, 0:512], pars[:, 512:1024]]
    prot = [pars[:, 1024:1536], pars[:, 1536:2048]]
    ptb = pars[:, 2048:2560].bitcast(BF16)
    pvB = [pars[:, i * 1024:(i + 1) * 1024] for i in range(4)]
    ptbB = pars[:, 0:512].bitcast(BF16)
    pS = [pars[:, 0:1024], pars[:, 1024:2048], pars[:, 3072:4096]]
    pAV = pars[:, 2048:3072]
    slotC = [pars[:, 3072:3137], pars[:, 3584:3649]]
    pT = [pars[:, 0:1024], pars[:, 1536:2560]]
    pR = [pars[:, 1024:1536], pars[:, 2560:3072]]
    ph = [pars[:, 0:1024], pars[:, 1024:2048]]
    pp = [pars[:, 2048:3072], pars[:, 3072:4096]]

    pr = Prog()
    for s, st in [("t", 1), ("v", 1), ("s", 1), ("g", 1), ("ds", 16),
                  ("do", 16), ("d_xt", 16), ("d_azt", 16), ("d_xqt", 16),
                  ("d_yt", 16), ("d_acc", 16), ("s_az0", 16), ("s_az1", 16),
                  ("s_y", 16), ("s_lnf", 16), ("d_az2", 16), ("d_xq2", 16), ("d_cos", 16), ("d_sin", 16), ("s_y2", 16), ("s_lnf2", 16),
                  ("g_a0", 16), ("g_a1", 16), ("g_a2", 16), ("g_a3", 16),
                  ("g_b0", 16), ("g_b1", 16),
                  ("g_f0", 16), ("g_f1", 16), ("g_f2", 16), ("g_f3", 16),
                  ("g_p0", 16), ("g_p1", 16)]:
        pr.sem(s, st)

    B = Buf
    b_xt = B(); b_lnT = B(); b_lnb = B(); b_wb0 = B(); b_wb1 = B()
    b_lnbs = [b_lnb, B()]; b_stats2 = [B(), B()]; b_junk = B()
    b_qraw2 = [B(), B()]; b_rt12 = [B(), B()]; b_rt22 = [B(), B()]
    b_Kf = B(); b_Qf = B(); b_Vau = B(); b_cos = B(); b_sin = B()
    b_consts = B()
    b_pq = [B(), B()]; b_prot = [B(), B()]; b_ptb = B()
    b_pv = [B(), B(), B(), B()]
    b_pS = [B(), B(), B()]; b_pAV = B(); b_Qz = [B(), B()]
    b_slotC = [b_pS[2], b_pS[2]]
    b_pT = B(); b_pR = B()
    b_ph = [B(), B()]; b_pp = [B(), B()]
    b_stats = B()
    b_atf = B(); b_az = [B(), B()]; b_rl = B()
    b_comb = B(); b_combrow = B()
    b_attnZ = B(); b_yscr = B(); b_lnfs = B()
    b_acc = B(); b_ln3f = B(); b_hf = B()
    b_wfcb = [B(), B(), B(), B()]; b_wpb = [B(), B()]; b_cbc = B(); b_tmpw = B()
    b_wbA = [B(), B(), B(), B()]
    b_out = B()
    b_r8 = B(); b_route = B()

    wb = [wb0, wb1]
    b_wb = [b_wb0, b_wb1]

    b_pb = [B(), B()]
    b_azt = B(); b_xqt = B(); b_yt = B(); b_tmpf = B()
    b_ytile = B(); b_ot = B(); b_yt2 = B()

    pr.do("vector", lambda e: e.memset(ones1[:], 1.0), writes=[b_consts])
    pr.do("sync", lambda e: e.dma_start(idf[:], idf_h), writes=[b_consts], sem="ds")
    pr.do("sync", lambda e: e.dma_start(rtm[:], rtm_h), writes=[b_consts], sem="ds")
    pr.do("sync", lambda e: e.dma_start(idbf[:], idbf_h), writes=[b_consts], sem="ds")
    pr.do("sync", lambda e: e.dma_start(
        centT[:].rearrange("p (c e) -> p c e", c=8),
        centT_h.rearrange("(c p) e -> p c e", p=128)), writes=[b_consts], sem="ds")
    pr.do("sync", lambda e: e.dma_start(rb8[:], rb8_h), writes=[b_consts], sem="ds")
    pr.do("sync", lambda e: e.dma_start(io8[:], io8_h), writes=[b_consts], sem="ds")
    pr.do("sync", lambda e: e.dma_start(cosb[:, 0:NQ], cosq_h), writes=[b_cos], sem="ds")
    pr.do("sync", lambda e: e.dma_start(sinb[:, 0:NQ], sinq_h), writes=[b_sin], sem="ds")
    b_consts.w = ("ds", pr.cnt["ds"])
    b_cos.w = ("ds", pr.cnt["ds"])
    b_sin.w = ("ds", pr.cnt["ds"])

    def ln_stats(src_ap, src_buf, par=0):
        bs = b_stats2[par]
        pr.do("scalar", lambda e: e.activation(
            junk[:], src_ap, AF.Square, accum_out=s2col[par][:]),
            reads=[src_buf], writes=[bs, b_junk])
        pr.do("vector", lambda e: e.reduce_sum(scol[par][:], src_ap, axis=AX.X),
              reads=[src_buf], writes=[bs])

    def ln_finish(src_ap, src_buf, out_ap, out_buf, par=0):
        bs = b_stats2[par]
        pr.do("vector", lambda e: e.tensor_scalar(
            mcol[par][:], scol[par][:], 1.0 / D, None, ALU.mult),
            reads=[bs], writes=[bs])
        pr.do("vector", lambda e: e.tensor_scalar(
            varc[par][:], s2col[par][:], 1.0 / D, EPS, ALU.mult, ALU.add),
            reads=[bs], writes=[bs])
        pr.do("vector", lambda e: e.tensor_mul(
            mm2[par][:], mcol[par][:], mcol[par][:]), reads=[bs], writes=[bs])
        pr.do("vector", lambda e: e.tensor_sub(
            varc[par][:], varc[par][:], mm2[par][:]), reads=[bs], writes=[bs])
        pr.do("scalar", lambda e: e.activation(varc[par][:], varc[par][:], AF.Ln),
              reads=[bs], writes=[bs])
        pr.do("scalar", lambda e: e.activation(
            rstdc[par][:], varc[par][:], AF.Exp, scale=-0.5),
            reads=[bs], writes=[bs])
        pr.do("vector", lambda e: e.tensor_scalar(
            negmr[par][:], mcol[par][:], rstdc[par][:], -1.0, ALU.mult, ALU.mult),
            reads=[bs], writes=[bs])
        pr.do("scalar", lambda e: e.activation(
            out_ap, src_ap, AF.Identity, bias=negmr[par][:], scale=rstdc[par][:]),
            reads=[src_buf, bs], writes=[out_buf])

    def ln_tile(src_ap, src_buf, out_ap, out_buf, par=0):
        ln_stats(src_ap, src_buf, par)
        ln_finish(src_ap, src_buf, out_ap, out_buf, par)

    def ln_transpose(j, ptb_ap, ptb_buf, par=0):
        jb = lnbs[par]
        fns = []
        for dc in range(8):
            fns.append(lambda e, dc=dc, jb=jb: e.transpose(
                ptb_ap[:, dc * 128:(dc + 1) * 128],
                jb[:, dc * 128:(dc + 1) * 128], idbf[:]))
        pr.group("tensor", fns, reads=[b_lnbs[par], b_consts], writes=[ptb_buf])
        dst = lnT[:].rearrange("p (c t) -> p c t", c=8)[:, :, j * 128:(j + 1) * 128]
        src = ptb_ap.rearrange("p (c t) -> p c t", c=8)
        pr.do("vector", lambda e: e.tensor_copy(dst, src),
              reads=[ptb_buf], writes=[b_lnT])

    def qk_pass(nchunks, x_src, wblk_h, dst_view, dst_buf, span, costab_off):
        for tc in range(nchunks):
            pr.do("sync", lambda e, tc=tc: e.dma_start(
                xt[:].rearrange("p (j d) -> p j d", j=4),
                x_src[tc * 512:(tc + 1) * 512, :].rearrange(
                    "(j p) d -> p j d", p=128)),
                writes=[b_xt], sem="d_xt")
            tiles = [xt[:, j * 1024:(j + 1) * 1024] for j in range(4)]
            stages = []
            for j in range(4):
                stages.append((lambda j=j: ln_stats(tiles[j], b_xt, par=j % 2),
                               lambda j=j: ln_finish(tiles[j], b_xt,
                                                     lnbs[j % 2][:],
                                                     b_lnbs[j % 2], par=j % 2),
                               lambda j=j: ln_transpose(j, ptb, b_ptb,
                                                        par=j % 2)))
            stages[0][0](); stages[1][0](); stages[0][1]()
            stages[2][0](); stages[1][1](); stages[0][2]()
            stages[3][0](); stages[2][1](); stages[1][2]()
            stages[3][1](); stages[2][2](); stages[3][2]()

            def w_dma(oc):
                w = oc % 4
                pr.do("gpsimd", lambda e, oc=oc, w=w: e.dma_start(
                    wbA[w][:].rearrange("p (c j) -> p c j", c=8),
                    wblk_h[oc].rearrange("c p j -> p c j")),
                    writes=[b_wbA[w]], sem="g_a%d" % w)

            def pg(oc):
                w = oc % 4
                fns = []
                for dc in range(8):
                    fns.append(lambda e, w=w, dc=dc, oc=oc: e.matmul(
                        pq[oc % 2][:, 0:512], wbA[w][:, dc * 128:(dc + 1) * 128],
                        lnT[:, dc * 512:(dc + 1) * 512],
                        start=(dc == 0), stop=(dc == 7), skip_group_check=True))
                pr.group("tensor", fns, reads=[b_wbA[w], b_lnT],
                         writes=[b_pq[oc % 2]])

            def rope(oc):
                w = oc % 2
                qr = [qraw, qrawB][w]; r1 = [rt1, rt1B][w]; r2 = rt2
                bqr = b_qraw2[w]; br1 = b_rt12[w]; br2 = b_rt22[0]
                pr.do("scalar", lambda e, w=w, qr=qr: e.activation(
                    qr[:], pq[w][:], AF.Identity), reads=[b_pq[w]], writes=[bqr])
                pr.do("tensor", lambda e, w=w, qr=qr: e.matmul(
                    prot[w][:], rtm[:], qr[:], start=True, stop=True),
                    reads=[bqr, b_consts], writes=[b_prot[w]])
                co = costab_off + tc * 512
                pr.do("vector", lambda e, w=w, co=co, r1=r1: e.tensor_mul(
                    r1[:], prot[w][:], sinb[:, co:co + 512]),
                    reads=[b_prot[w], b_sin], writes=[br1])
                pr.do("gpsimd", lambda e, w=w, co=co, qr=qr, r2=r2: e.tensor_mul(
                    r2[:], qr[:], cosb[:, co:co + 512]),
                    reads=[bqr, b_cos], writes=[br2])
                o = oc * span + tc * 512
                pr.do("vector", lambda e, o=o, r1=r1, r2=r2: e.tensor_add(
                    dst_view[:, o:o + 512], r1[:], r2[:]),
                    reads=[br1, br2], writes=[dst_buf])

            w_dma(0); w_dma(1); w_dma(2)
            pg(0)
            for oc in range(8):
                if oc + 3 < 8:
                    w_dma(oc + 3)
                if oc + 1 < 8:
                    pg(oc + 1)
                rope(oc)

    qk_pass(4, xq_h, wqb_h, Q_f, b_Qf, NQ, 0)
    pr.do("sync", lambda e: e.dma_start(cosb[:], cosk_h), writes=[b_cos], sem="d_cos")
    pr.do("sync", lambda e: e.dma_start(sinb[:], sink_h), writes=[b_sin], sem="d_sin")
    qk_pass(8, xkv_h, wkb_h, K_f, b_Kf, NK, 0)

    pr.fence()

    lnbs[1] = qraw[:].bitcast(BF16)
    b_lnbs[1] = b_qraw2[0]
    pr.do("vector", lambda e: e.memset(V_au[:], 1.0), writes=[b_Vau])
    for tc in range(8):
        pr.do("sync", lambda e, tc=tc: e.dma_start(
            xt[:].rearrange("p (j d) -> p j d", j=4),
            xkv_h[tc * 512:(tc + 1) * 512, :].rearrange("(j p) d -> p j d", p=128)),
            writes=[b_xt], sem="d_xt")
        tiles = [xt[:, j * 1024:(j + 1) * 1024] for j in range(4)]
        stages = []
        for j in range(4):
            stages.append((lambda j=j: ln_stats(tiles[j], b_xt, par=j % 2),
                           lambda j=j: ln_finish(tiles[j], b_xt,
                                                 lnbs[j % 2][:],
                                                 b_lnbs[j % 2], par=j % 2),
                           lambda j=j: ln_transpose(j, ptbB, b_pv[0],
                                                    par=j % 2)))
        stages[0][0](); stages[1][0](); stages[0][1]()
        stages[2][0](); stages[1][1](); stages[0][2]()
        stages[3][0](); stages[2][1](); stages[1][2]()
        stages[3][1](); stages[2][2](); stages[3][2]()
        for dc in range(8):
            w = dc % 2
            pr.do("gpsimd", lambda e, dc=dc, w=w: e.dma_start(
                wb[w][:], wvT_h[dc * 128:(dc + 1) * 128, :]),
                writes=[b_wb[w]], sem="g_b%d" % w)
            for j in range(4):
                fns = []
                for nh in range(2):
                    fns.append(lambda e, j=j, dc=dc, w=w, nh=nh: e.matmul(
                        pvB[j][:, nh * 512:(nh + 1) * 512],
                        lnT[:, dc * 512 + j * 128: dc * 512 + (j + 1) * 128],
                        wb[w][:, nh * 512:(nh + 1) * 512],
                        start=(dc == 0), stop=(dc == 7), skip_group_check=True))
                pr.group("tensor", fns, reads=[b_wb[w], b_lnT], writes=[b_pv[j]])
        for j in range(4):
            kt = tc * 4 + j
            dst = V_au[:, kt * 1056:(kt + 1) * 1056].rearrange(
                "p (h x) -> p h x", h=16)[:, :, 0:64]
            src = pvB[j][:].rearrange("p (h x) -> p h x", h=16)
            pr.do("vector", lambda e, dst=dst, src=src: e.tensor_copy(dst, src),
                  reads=[b_pv[j]], writes=[b_Vau])

    pr.fence()

    pr.do("vector", lambda e: e.memset(Qz[0][:], 0.0), writes=[b_Qz[0]])
    pr.do("vector", lambda e: e.memset(Qz[1][:], 0.0), writes=[b_Qz[1]])

    def c_unit(h, qh2):
        hp = (h % 2) * 64
        ho = h // 2
        par = h % 2
        qbase = ho * NQ + qh2 * 1024

        def qz_copy():
            pr.do("vector", lambda e, hp=hp, qbase=qbase, par=par: e.tensor_copy(
                Qz[par][hp:hp + 64, :], Q_f[hp:hp + 64, qbase:qbase + 1024]),
                reads=[b_Qf], writes=[b_Qz[par]])

        def emit_S(kt):
            w = kt % 3
            fns = []
            for nh in range(2):
                o_ap = pS[w][:, nh * 512:(nh + 1) * 512]
                l_ap = K_f[0:128, ho * NK + kt * 128: ho * NK + (kt + 1) * 128]
                r_ap = Qz[par][:, nh * 512:nh * 512 + 512]
                fns.append(lambda e, o=o_ap, l=l_ap, r=r_ap: e.matmul(
                    o, l, r, start=True, stop=True, skip_group_check=True))
            pr.group("tensor", fns, reads=[b_Kf, b_Qz[par]], writes=[b_pS[w]])

        def emit_X(kt):
            w = kt % 3
            pbuf = pb0 if kt % 2 == 0 else pb1
            pr.do("scalar", lambda e, w=w, pbuf=pbuf: e.activation(
                pbuf[:], pS[w][:], AF.Exp, scale=0.125),
                reads=[b_pS[w]], writes=[b_pb[kt % 2]])

        def emit_AV(kt):
            pbuf = pb0 if kt % 2 == 0 else pb1
            fns = []
            for nh in range(2):
                o_ap = pAV[0:65, nh * 512:(nh + 1) * 512]
                l_ap = V_au[:, kt * 1056 + h * 66: kt * 1056 + h * 66 + 65]
                r_ap = pbuf[:, nh * 512:(nh + 1) * 512]
                fns.append(lambda e, o=o_ap, l=l_ap, r=r_ap, kt=kt: e.matmul(
                    o, l, r, start=(kt == 0), stop=(kt == 31),
                    skip_group_check=True))
            pr.group("tensor", fns, reads=[b_Vau, b_pb[kt % 2]], writes=[b_pAV])

        def finalize():
            ab = qh2 * 1024
            pr.do("vector", lambda e, ab=ab: e.tensor_copy(
                atf[0:65, ab:ab + 1024], pAV[0:65, :]),
                reads=[b_pAV], writes=[b_atf])
            fin = []
            for q8 in range(8):
                qt = qh2 * 8 + q8
                sl_ap = slotC[q8 % 2]
                sl_buf = b_slotC[q8 % 2]

                def fin_one(qt=qt, sl_ap=sl_ap, sl_buf=sl_buf, h=h):
                    pr.do("tensor", lambda e, qt=qt, sl_ap=sl_ap: e.transpose(
                        sl_ap, atf[0:65, qt * 128:(qt + 1) * 128],
                        idf[0:65, 0:65]),
                        reads=[b_atf, b_consts], writes=[sl_buf])
                    pr.do("vector", lambda e, sl_ap=sl_ap: e.reciprocal(
                        rl[:], sl_ap[:, 64:65]), reads=[sl_buf], writes=[b_rl])
                    azb = [az0, az1][qt % 2]
                    pr.do("vector",
                          lambda e, sl_ap=sl_ap, azb=azb: e.tensor_scalar(
                              azb[:], sl_ap[:, 0:64], rl[:], None, ALU.mult),
                          reads=[sl_buf, b_rl], writes=[b_az[qt % 2]])
                    pr.do("sync", lambda e, qt=qt, h=h, azb=azb: e.dma_start(
                        attnZ_h[qt * 128:(qt + 1) * 128,
                                h * 64:(h + 1) * 64], azb[:]),
                        reads=[b_az[qt % 2]], sem="s_az%d" % (qt % 2))
                fin.append(fin_one)
            return fin

        return qz_copy, emit_S, emit_X, emit_AV, finalize

    pending_fin = []
    for h in range(16):
        for qh2 in range(2):
            qz_copy, emit_S, emit_X, emit_AV, finalize = c_unit(h, qh2)
            qz_copy()
            emit_S(0); emit_X(0); emit_S(1); emit_X(1)
            for f in pending_fin:
                f()
            pending_fin = []
            for kt in range(32):
                emit_AV(kt)
                if kt + 2 < 32:
                    emit_S(kt + 2)
                    emit_X(kt + 2)
            pending_fin = finalize()
    for f in pending_fin:
        f()

    pr.fence()

    AZT = [azt, azt1]; XQT = [xqt, xqt1]; YTT = [yt, yt1]; TMPF = [tmpf, tmpf1]
    b_AZT = [b_azt, B()]; b_XQT = [b_xqt, B()]; b_YTT = [b_yt, B()]
    b_TMPF = [b_tmpf, B()]
    b_pT2 = [b_pT, B()]; b_pR2 = [b_pR, B()]
    b_rt2 = [b_route, B()]; b_r82 = [b_r8, B()]
    for j in range(16):
        p = j % 2
        az_, xq_, yy_, tf_ = AZT[p], XQT[p], YTT[p], TMPF[p]
        baz, bxq, byy, btf = b_AZT[p], b_XQT[p], b_YTT[p], b_TMPF[p]
        bt, brt = b_pT2[p], b_rt2[p]
        pTp, pRp = pT[p], pR[p]
        br8 = b_r82[p]
        pr.do("sync", lambda e, j=j, az_=az_: e.dma_start(
            az_, attnZ_h[j * 128:(j + 1) * 128, :]), writes=[baz],
            sem="d_azt" if p == 0 else "d_az2")
        pr.do("sync", lambda e, j=j, xq_=xq_: e.dma_start(
            xq_, xq_h[j * 128:(j + 1) * 128, :]), writes=[bxq],
            sem="d_xqt" if p == 0 else "d_xq2")
        ln_tile(az_, baz, tf_, btf, par=p)
        pr.do("vector", lambda e, yy_=yy_, xq_=xq_, tf_=tf_: e.tensor_add(
            yy_, xq_, tf_), reads=[bxq, btf], writes=[byy])
        pr.do("scalar", lambda e, j=j, yy_=yy_: e.dma_start(
            yscr_h[j * 128:(j + 1) * 128, :], yy_), reads=[byy],
            sem="s_y" if p == 0 else "s_y2")
        ln_tile(yy_, byy, tf_, btf, par=p)
        fns = []
        for dc in range(8):
            fns.append(lambda e, dc=dc, pTp=pTp, tf_=tf_: e.transpose(
                pTp[:, dc * 128:(dc + 1) * 128],
                tf_[:, dc * 128:(dc + 1) * 128], idf[:]))
        pr.group("tensor", fns, reads=[btf, b_consts], writes=[bt])
        pr.do("vector", lambda e, az_=az_, pTp=pTp: e.tensor_copy(az_, pTp[:]),
              reads=[bt], writes=[baz])
        pr.do("scalar", lambda e, j=j, az_=az_: e.dma_start(lnf_h[j], az_),
              reads=[baz], sem="s_lnf" if p == 0 else "s_lnf2")
        dst = ln3f[:].rearrange("p (c t) -> p c t", c=8)[:, :, j * 128:(j + 1) * 128]
        srcv = az_.rearrange("p (c t) -> p c t", c=8)
        pr.do("vector", lambda e, dst=dst, srcv=srcv: e.tensor_copy(dst, srcv),
              reads=[baz], writes=[b_ln3f])
        fns = []
        for dc in range(8):
            fns.append(lambda e, dc=dc, pRp=pRp, az_=az_: e.matmul(
                pRp[0:8, 0:128], centT[:, dc * 8:(dc + 1) * 8],
                az_[:, dc * 128:(dc + 1) * 128],
                start=(dc == 0), stop=(dc == 7), skip_group_check=True))
        pr.group("tensor", fns, reads=[baz, b_consts], writes=[b_pR2[p]])
        pr.do("vector", lambda e, p=p, pRp=pRp: e.tensor_copy(
            r8[p][:], pRp[0:8, 0:128]), reads=[b_pR2[p]], writes=[br8])
        pr.do("tensor", lambda e, p=p, pRp=pRp: e.transpose(
            pRp[:, 128:136], r8[p][:], idf[0:8, 0:8]),
            reads=[br8, b_consts], writes=[b_pR2[p]])
        pr.do("vector", lambda e, p=p, pRp=pRp: e.tensor_copy(
            raw8[p][:], pRp[:, 128:136]), reads=[b_pR2[p]], writes=[brt])
        pr.do("scalar", lambda e, p=p: e.activation(
            sc8[p][:], raw8[p][:], AF.Sigmoid), reads=[brt], writes=[brt])
        pr.do("vector", lambda e, p=p: e.tensor_add(
            bal8[p][:], raw8[p][:], rb8[:]), reads=[brt, b_consts], writes=[brt])
        pr.do("scalar", lambda e, p=p: e.activation(
            bal8b[p][:], bal8[p][:], AF.Sigmoid), reads=[brt], writes=[brt])
        pr.do("vector", lambda e, p=p: e.reduce_max(
            m1c[p][:], bal8b[p][:], axis=AX.X), reads=[brt], writes=[brt])
        pr.do("vector", lambda e, p=p: e.tensor_single_scalar(
            eq8[p][:], bal8b[p][:], m1c[p][:], ALU.is_ge),
            reads=[brt], writes=[brt])
        pr.do("vector", lambda e, p=p: e.scalar_tensor_tensor(
            mi8[p][:], eq8[p][:], -1024.0, io8[:], ALU.mult, ALU.add),
            reads=[brt, b_consts], writes=[brt])
        pr.do("vector", lambda e, p=p: e.tensor_reduce(
            i1c[p][:], mi8[p][:], axis=AX.X, op=ALU.min),
            reads=[brt], writes=[brt])
        pr.do("vector", lambda e, p=p: e.tensor_scalar(
            i1c[p][:], i1c[p][:], 1024.0, None, ALU.add),
            reads=[brt], writes=[brt])
        pr.do("vector", lambda e, p=p: e.tensor_single_scalar(
            oh1[p][:], io8[:], i1c[p][:], ALU.is_equal),
            reads=[brt, b_consts], writes=[brt])
        pr.do("vector", lambda e, p=p: e.tensor_mul(
            tmp8[p][:], oh1[p][:], sc8[p][:]), reads=[brt], writes=[brt])
        pr.do("vector", lambda e, p=p: e.reduce_sum(
            g1c[p][:], tmp8[p][:], axis=AX.X), reads=[brt], writes=[brt])
        pr.do("vector", lambda e, p=p: e.scalar_tensor_tensor(
            bal2[p][:], oh1[p][:], -1024.0, bal8b[p][:], ALU.mult, ALU.add),
            reads=[brt], writes=[brt])
        pr.do("vector", lambda e, p=p: e.reduce_max(
            m1c[p][:], bal2[p][:], axis=AX.X), reads=[brt], writes=[brt])
        pr.do("vector", lambda e, p=p: e.tensor_single_scalar(
            eq8[p][:], bal2[p][:], m1c[p][:], ALU.is_ge),
            reads=[brt], writes=[brt])
        pr.do("vector", lambda e, p=p: e.scalar_tensor_tensor(
            mi8[p][:], eq8[p][:], -1024.0, io8[:], ALU.mult, ALU.add),
            reads=[brt, b_consts], writes=[brt])
        pr.do("vector", lambda e, p=p: e.tensor_reduce(
            i1c[p][:], mi8[p][:], axis=AX.X, op=ALU.min),
            reads=[brt], writes=[brt])
        pr.do("vector", lambda e, p=p: e.tensor_scalar(
            i1c[p][:], i1c[p][:], 1024.0, None, ALU.add),
            reads=[brt], writes=[brt])
        pr.do("vector", lambda e, p=p: e.tensor_single_scalar(
            oh2[p][:], io8[:], i1c[p][:], ALU.is_equal),
            reads=[brt, b_consts], writes=[brt])
        pr.do("vector", lambda e, p=p: e.tensor_mul(
            tmp8[p][:], oh2[p][:], sc8[p][:]), reads=[brt], writes=[brt])
        pr.do("vector", lambda e, p=p: e.reduce_sum(
            g2c[p][:], tmp8[p][:], axis=AX.X), reads=[brt], writes=[brt])
        pr.do("vector", lambda e, p=p: e.tensor_sub(
            d12[p][:], g2c[p][:], g1c[p][:]), reads=[brt], writes=[brt])
        pr.do("scalar", lambda e, p=p: e.activation(
            w1c[p][:], d12[p][:], AF.Sigmoid, scale=-1.0),
            reads=[brt], writes=[brt])
        pr.do("scalar", lambda e, p=p: e.activation(
            w2c[p][:], d12[p][:], AF.Sigmoid, scale=1.0),
            reads=[brt], writes=[brt])
        pr.do("vector", lambda e, p=p: e.tensor_scalar(
            tmp8[p][:], oh1[p][:], w1c[p][:], None, ALU.mult),
            reads=[brt], writes=[brt])
        pr.do("vector", lambda e, j=j, p=p: e.scalar_tensor_tensor(
            comb[:, j * 8:(j + 1) * 8], oh2[p][:], w2c[p][:], tmp8[p][:],
            ALU.mult, ALU.add), reads=[brt], writes=[b_comb])

    pr.fence()

    for tt in range(2):
        src = lnf_h[tt * 8:(tt + 1) * 8].rearrange("j p (c t) -> p c j t", c=8)
        dst = acc[:].rearrange("p (c j t) -> p c j t", c=8, j=8)
        pr.do("sync", lambda e, dst=dst, src=src: e.dma_start(dst, src),
              writes=[b_acc], sem="d_acc")
        for ei, e_ in enumerate([8, 9, 0, 1, 2, 3, 4, 5, 6, 7]):
            routed = e_ < 8
            if routed:
                fns = []
                for j8 in range(8):
                    jg = tt * 8 + j8
                    o_ap = ph[0][0:1, j8 * 128:(j8 + 1) * 128]
                    i_ap = comb[:, jg * 8 + e_: jg * 8 + e_ + 1]
                    fns.append(lambda e, o=o_ap, i=i_ap: e.transpose(
                        o, i, idf[:]))
                pr.group("tensor", fns, reads=[b_comb, b_consts], writes=[b_ph[0]])
                pr.do("vector", lambda e: e.tensor_copy(
                    combrow_w, ph[0][0:1, 0:1024]),
                    reads=[b_ph[0]], writes=[b_tmpw])
                fns = []
                for j8 in range(8):
                    o_ap = ph[1][:, j8 * 128:(j8 + 1) * 128]
                    r_ap = tmpw[0:1, j8 * 128:(j8 + 1) * 128]
                    fns.append(lambda e, o=o_ap, r=r_ap: e.matmul(
                        o, ones1[:], r, start=True, stop=True,
                        skip_group_check=True))
                pr.group("tensor", fns, reads=[b_tmpw, b_consts], writes=[b_ph[1]])
                pr.do("vector", lambda e: e.tensor_copy(cbc[:], ph[1][:]),
                      reads=[b_ph[1]], writes=[b_cbc])
            for fc in range(32):
                w = fc % 4
                pr.do("gpsimd", lambda e, e_=e_, fc=fc, w=w: e.dma_start(
                    wfcb[w][:].rearrange("p (c j) -> p c j", c=8),
                    wfcT_h[e_, fc].rearrange("c p j -> p c j")),
                    writes=[b_wfcb[w]], sem="g_f%d" % w)
                fns = []
                for dc in range(8):
                    for nh in range(2):
                        fns.append(lambda e, w=w, dc=dc, nh=nh, tt=tt: e.matmul(
                            ph[w % 2][:, nh * 512:(nh + 1) * 512],
                            wfcb[w][:, dc * 128:(dc + 1) * 128],
                            ln3f[:, dc * 2048 + tt * 1024 + nh * 512:
                                 dc * 2048 + tt * 1024 + nh * 512 + 512],
                            start=(dc == 0), stop=(dc == 7), skip_group_check=True))
                pr.group("tensor", fns, reads=[b_wfcb[w], b_ln3f],
                         writes=[b_ph[w % 2]])
                pr.do("scalar", lambda e, fc=fc, w=w: e.activation(
                    h_f[:, fc * 1024:(fc + 1) * 1024], ph[w % 2][:], AF.Gelu),
                    reads=[b_ph[w % 2]], writes=[b_hf])
            for dci in range(8):
                w = dci % 2
                pr.do("gpsimd", lambda e, e_=e_, dci=dci, w=w: e.dma_start(
                    wpb[w][:].rearrange("p (c j) -> p c j", c=32),
                    wprojT_h[e_, dci].rearrange("c p j -> p c j")),
                    writes=[b_wpb[w]], sem="g_p%d" % w)
                fns = []
                for fc in range(32):
                    for nh in range(2):
                        fns.append(lambda e, w=w, fc=fc, nh=nh: e.matmul(
                            pp[w][:, nh * 512:(nh + 1) * 512],
                            wpb[w][:, fc * 128:(fc + 1) * 128],
                            h_f[:, fc * 1024 + nh * 512: fc * 1024 + nh * 512 + 512],
                            start=(fc == 0), stop=(fc == 31), skip_group_check=True))
                pr.group("tensor", fns, reads=[b_wpb[w], b_hf], writes=[b_pp[w]])
                a_sl = acc[:, dci * 1024:(dci + 1) * 1024]
                if routed:
                    pr.do("vector", lambda e, w=w: e.tensor_mul(
                        tmpw[:], pp[w][:], cbc[:]),
                        reads=[b_pp[w], b_cbc], writes=[b_tmpw])
                    pr.do("vector", lambda e, a_sl=a_sl: e.tensor_add(
                        a_sl, a_sl, tmpw[:]), reads=[b_tmpw], writes=[b_acc])
                else:
                    pr.do("vector", lambda e, a_sl=a_sl, w=w: e.tensor_add(
                        a_sl, a_sl, pp[w][:]), reads=[b_pp[w]], writes=[b_acc])
        for j8 in range(8):
            w = j8 % 2
            fns = []
            for dc in range(8):
                fns.append(lambda e, w=w, dc=dc, j8=j8: e.transpose(
                    pp[w][:, dc * 128:(dc + 1) * 128],
                    acc[:, dc * 1024 + j8 * 128: dc * 1024 + (j8 + 1) * 128],
                    idf[:]))
            pr.group("tensor", fns, reads=[b_acc, b_consts], writes=[b_pp[w]])
            t_sq = pr.do("scalar", lambda e, w=w: e.activation(
                junk[:], pp[w][:], AF.Square, accum_out=s2col[0][:]),
                reads=[b_pp[w]], writes=[b_stats, b_junk])
            pr.do("vector", lambda e, w=w: e.reduce_sum(
                scol[0][:], pp[w][:], axis=AX.X),
                reads=[b_pp[w]], writes=[b_stats], after=[t_sq])
            pr.do("vector", lambda e: e.tensor_scalar(
                mcol[0][:], scol[0][:], 1.0 / D, None, ALU.mult),
                reads=[b_stats], writes=[b_stats])
            pr.do("vector", lambda e: e.tensor_scalar(
                varc[0][:], s2col[0][:], 1.0 / D, EPS, ALU.mult, ALU.add),
                reads=[b_stats], writes=[b_stats])
            pr.do("vector", lambda e: e.tensor_mul(
                mm2[0][:], mcol[0][:], mcol[0][:]),
                reads=[b_stats], writes=[b_stats])
            pr.do("vector", lambda e: e.tensor_sub(
                varc[0][:], varc[0][:], mm2[0][:]),
                reads=[b_stats], writes=[b_stats])
            pr.do("scalar", lambda e: e.activation(varc[0][:], varc[0][:], AF.Ln),
                  reads=[b_stats], writes=[b_stats])
            pr.do("scalar", lambda e: e.activation(
                rstdc[0][:], varc[0][:], AF.Exp, scale=-0.5),
                reads=[b_stats], writes=[b_stats])
            pr.do("vector", lambda e: e.tensor_scalar(
                negmr[0][:], mcol[0][:], rstdc[0][:], -1.0, ALU.mult, ALU.mult),
                reads=[b_stats], writes=[b_stats])
            row = tt * 1024 + j8 * 128
            pr.do("sync", lambda e, row=row: e.dma_start(
                ytile, yscr_h[row:row + 128, :]),
                writes=[b_ytile], sem="d_yt")
            pr.do("vector", lambda e: e.tensor_scalar(
                yt2, ytile, negmr[0][:], None, ALU.add),
                reads=[b_ytile, b_stats], writes=[b_yt2])
            pr.do("vector", lambda e, w=w: e.scalar_tensor_tensor(
                ot, pp[w][:], rstdc[0][:], yt2, ALU.mult, ALU.add),
                reads=[b_pp[w], b_yt2, b_stats], writes=[b_ot])
            pr.do("scalar", lambda e, row=row: e.dma_start(
                out_h[row:row + 128, :], ot), reads=[b_ot], sem="do")

    pr.fence()

    import contextlib
    with contextlib.ExitStack() as st:
        handles = {}
        for s in pr.cnt:
            handles[s] = st.enter_context(nc.semaphore(s))
        pr.finalize(nc, handles)
    return nc


def _rope_tables():
    pos = np.arange(NK, dtype=np.float32)
    inv = np.power(np.float32(10000.0),
                   np.float32(-2.0) * np.arange(HD // 2, dtype=np.float32)
                   / np.float32(HD))
    ang = pos[:, None] * inv[None, :]
    cos = np.repeat(np.cos(ang), 2, axis=-1)
    sin = np.repeat(np.sin(ang), 2, axis=-1)
    cosT = np.ascontiguousarray(np.vstack([cos.T, cos.T]).astype(np.float32))
    sinT = np.ascontiguousarray(np.vstack([sin.T, sin.T]).astype(np.float32))
    return cosT, sinT


def _rot_matrix():
    r64 = np.zeros((64, 64), np.float32)
    for i in range(32):
        r64[2 * i, 2 * i + 1] = -1.0
        r64[2 * i + 1, 2 * i] = 1.0
    r = np.zeros((128, 128), np.float32)
    r[0:64, 0:64] = r64
    r[64:128, 64:128] = r64
    return np.ascontiguousarray(r.T)


def kernel(L0_attn_x, L0_attn_Wq, L0_attn_Wk, L0_attn_Wv, L0_ffn_centroids,
           L0_ffn_routing_bias, L0_ffn_Wfc_r, L0_ffn_Wproj_r, L0_ffn_Wfc_sh,
           L0_ffn_Wproj_sh):
    x = np.ascontiguousarray(np.asarray(L0_attn_x, dtype=np.float32))
    Wq = np.asarray(L0_attn_Wq, dtype=np.float32)
    Wk = np.asarray(L0_attn_Wk, dtype=np.float32)
    Wv = np.asarray(L0_attn_Wv, dtype=np.float32)
    cent = np.asarray(L0_ffn_centroids, dtype=np.float32)
    rbias = np.asarray(L0_ffn_routing_bias, dtype=np.float32)
    WfcAll = np.concatenate([np.asarray(L0_ffn_Wfc_r, np.float32),
                             np.asarray(L0_ffn_Wfc_sh, np.float32)], axis=0)
    WprojAll = np.concatenate([np.asarray(L0_ffn_Wproj_r, np.float32),
                               np.asarray(L0_ffn_Wproj_sh, np.float32)], axis=0)

    wqb = np.ascontiguousarray(
        Wq.T.reshape(8, 128, 8, 128).transpose(2, 0, 1, 3))
    wkb = np.ascontiguousarray(
        Wk.T.reshape(8, 128, 8, 128).transpose(2, 0, 1, 3))
    wvT = np.ascontiguousarray(Wv.T)
    centT = np.ascontiguousarray(cent.T)
    rb8 = np.ascontiguousarray(np.tile(rbias[None, :], (128, 1)).astype(np.float32))
    io8 = np.ascontiguousarray(
        np.tile(np.arange(8, dtype=np.float32)[None, :], (128, 1)))
    rtm = _rot_matrix()
    idf = np.eye(128, dtype=np.float32)
    idbf = np.eye(128, dtype=ml_dtypes.bfloat16)
    cosT, sinT = _rope_tables()
    wfcT = np.ascontiguousarray(
        WfcAll.reshape(NE, 32, 128, 8, 128).transpose(0, 1, 3, 4, 2))
    wprojT = np.ascontiguousarray(
        WprojAll.reshape(NE, 8, 128, 32, 128).transpose(0, 1, 3, 4, 2))

    nc = build_nc()
    in_maps = []
    for c in range(8):
        b, hf = c // 2, c % 2
        in_maps.append({
            "xq": np.ascontiguousarray(x[b, hf * NQ:(hf + 1) * NQ]),
            "xkv": x[b],
            "wqb": wqb, "wkb": wkb, "wvT": wvT,
            "centT": centT, "rb8": rb8, "io8": io8,
            "rtm": rtm, "idf": idf, "idbf": idbf,
            "cosq": np.ascontiguousarray(cosT[:, hf * NQ:(hf + 1) * NQ]),
            "sinq": np.ascontiguousarray(sinT[:, hf * NQ:(hf + 1) * NQ]),
            "cosk": cosT, "sink": sinT,
            "wfcT": wfcT, "wprojT": wprojT,
        })
    res = run_bass_kernel_spmd(nc, in_maps, core_ids=list(range(8)))
    out = np.empty((4, NK, D), dtype=np.float32)
    for c in range(8):
        b, hf = c // 2, c % 2
        out[b, hf * NQ:(hf + 1) * NQ] = np.asarray(res.results[c]["out"])
    return out


# version 25
# speedup vs baseline: 1.2104x; 1.2104x over previous
import numpy as np
import ml_dtypes
import concourse.bass as bass
import concourse.mybir as mybir
from concourse.bass_utils import run_bass_kernel_spmd

F32 = mybir.dt.float32
BF16 = mybir.dt.bfloat16
AF = mybir.ActivationFunctionType
ALU = mybir.AluOpType
AX = mybir.AxisListType

D = 1024
H = 16
HD = 64
NK = 4096
NQ = 2048
F = 4096
NE = 10
EPS = 1e-5


class Buf:
    __slots__ = ("w", "rs")

    def __init__(self):
        self.w = None
        self.rs = []


class Prog:
    ENGINES = ("tensor", "vector", "scalar", "gpsimd", "sync")

    def __init__(self):
        self.q = {e: [] for e in self.ENGINES}
        self.cnt = {}
        self.step = {}
        self.hwm = {}

    def sem(self, name, step):
        self.cnt[name] = 0
        self.step[name] = step

    def _wait(self, eng, tok):
        s, v = tok
        if v <= 0 or v <= self.hwm.get((eng, s), 0):
            return
        self.hwm[(eng, s)] = v
        self.q[eng].append(("w", s, v))

    def do(self, eng, fn, reads=(), writes=(), sem=None, after=()):
        toks = []
        for b in reads:
            if b.w:
                toks.append(b.w)
        for b in writes:
            if b.w:
                toks.append(b.w)
            toks.extend(b.rs)
        toks.extend(after)
        for t in sorted(set(toks)):
            self._wait(eng, t)
        s = sem if sem is not None else eng[0]
        self.cnt[s] += self.step[s]
        tok = (s, self.cnt[s])
        self.q[eng].append(("op", fn, s, self.step[s]))
        for b in writes:
            b.w = tok
            b.rs = []
        for b in reads:
            b.rs.append(tok)
        return tok

    def group(self, eng, fns, reads=(), writes=(), sem=None, after=()):
        toks = []
        for b in reads:
            if b.w:
                toks.append(b.w)
        for b in writes:
            if b.w:
                toks.append(b.w)
            toks.extend(b.rs)
        toks.extend(after)
        for t in sorted(set(toks)):
            self._wait(eng, t)
        s = sem if sem is not None else eng[0]
        for fn in fns[:-1]:
            self.q[eng].append(("op", fn, None, 0))
        self.cnt[s] += self.step[s]
        tok = (s, self.cnt[s])
        self.q[eng].append(("op", fns[-1], s, self.step[s]))
        for b in writes:
            b.w = tok
            b.rs = []
        for b in reads:
            b.rs.append(tok)
        return tok

    def fence(self):
        for e in self.ENGINES:
            for s, v in self.cnt.items():
                self._wait(e, (s, v))

    def finalize(self, nc, handles):
        with nc.Block() as blk:
            for ename in self.ENGINES:
                items = self.q[ename]
                if not items:
                    continue

                def run(eng, items=items):
                    for it in items:
                        if it[0] == "w":
                            eng.wait_ge(handles[it[1]], it[2])
                        else:
                            inst = it[1](eng)
                            if it[2] is not None:
                                inst.then_inc(handles[it[2]], it[3])
                getattr(blk, ename)(run)


def build_nc(dbg=None):
    nc = bass.Bass("TRN2", target_bir_lowering=False, debug=False)

    def din(name, shape, dt=F32):
        return nc.dram_tensor(name, shape, dt, kind="ExternalInput").ap()

    xq_h = din("xq", [NQ, D])
    xkv_h = din("xkv", [NK, D])
    wqb_h = din("wqb", [8, 8, 128, 128])
    wkb_h = din("wkb", [8, 8, 128, 128])
    wvT_h = din("wvT", [D, D])
    centT_h = din("centT", [D, 8])
    rb8_h = din("rb8", [128, 8])
    io8_h = din("io8", [128, 8])
    rtm_h = din("rtm", [128, 128])
    idf_h = din("idf", [128, 128])
    idbf_h = din("idbf", [128, 128], BF16)
    cosq_h = din("cosq", [128, NQ])
    sinq_h = din("sinq", [128, NQ])
    cosk_h = din("cosk", [128, NK])
    sink_h = din("sink", [128, NK])
    wfcT_h = din("wfcT", [NE, 32, 8, 128, 128])
    wprojT_h = din("wprojT", [NE, 8, 32, 128, 128])

    out_h = nc.dram_tensor("out", [NQ, D], F32, kind="ExternalOutput").ap()

    attnZ_h = nc.dram_tensor("attnZ", [NQ, D], F32, kind="Internal").ap()
    yscr_h = nc.dram_tensor("yscr", [NQ, D], F32, kind="Internal").ap()
    lnf_h = nc.dram_tensor("lnf", [16, 128, D], F32, kind="Internal").ap()

    arena = nc.alloc_sbuf_tensor("arena", [128, 41472], F32)

    def AV(off4, n_elems, dt):
        if dt == F32:
            return arena[:, off4:off4 + n_elems]
        return arena[:, off4:off4 + n_elems // 2].bitcast(dt)

    K_f = AV(0, 32768, BF16)
    Q_f = AV(16384, 16384, BF16)
    V_au = AV(24576, 33792, BF16)
    cosb = AV(24576, 4096, F32)
    sinb = AV(28672, 4096, F32)
    acc = AV(0, 8192, F32)
    ln3f = AV(8192, 16384, BF16)
    h_f = AV(16384, 32768, BF16)
    wfcb = [AV(32768, 1024, BF16), AV(33280, 1024, BF16),
            AV(39936, 1024, BF16), AV(40448, 1024, BF16)]
    wpb = [AV(33792, 4096, BF16), AV(35840, 4096, BF16)]
    cbc = AV(37888, 1024, F32)
    tmpw = AV(38912, 1024, F32)
    combrow_w = tmpw[0:1, 0:1024]
    qrawB = AV(39936, 512, F32)
    rt1B = AV(40448, 512, F32)
    wbA = [AV(32768, 1024, BF16), AV(33280, 1024, BF16),
           AV(33792, 1024, BF16), AV(34304, 1024, BF16)]
    lnb1 = AV(40960, 1024, BF16)
    azt1 = AV(16384, 1024, F32)
    xqt1 = AV(17408, 1024, F32)
    yt1 = AV(18432, 1024, F32)
    tmpf1 = AV(19456, 1024, F32)

    def sb(name, shape, dt=F32):
        return nc.alloc_sbuf_tensor(name, shape, dt)

    xt = sb("xt", [128, 4096])
    pb0 = xt[:, 0:512].bitcast(BF16)
    pb1 = xt[:, 512:1024].bitcast(BF16)
    Qz = [xt[:, 1024:1536].bitcast(BF16), xt[:, 1536:2048].bitcast(BF16)]
    atf = xt[:, 2048:4096]
    azt = xt[:, 0:1024]
    xqt = xt[:, 1024:2048]
    yt = xt[:, 2048:3072]
    tmpf = xt[:, 3072:4096]
    ytile = xt[:, 0:1024]
    ot = xt[:, 1024:2048]
    yt2 = xt[:, 2048:3072]

    lnT = sb("lnT", [128, 4096], BF16)
    lnb = sb("lnb", [128, 1024], BF16)
    wb0 = sb("wb0", [128, 1024], BF16)
    wb1 = sb("wb1", [128, 1024], BF16)
    qraw = sb("qraw", [128, 512])
    rt1 = sb("rt1", [128, 512])
    rt2 = sb("rt2", [128, 512])
    rtm = sb("rtm_t", [128, 128])
    idf = sb("idf_t", [128, 128])
    idbf = sb("idbf_t", [128, 128], BF16)
    centT = sb("centT_t", [128, 64])
    rb8 = sb("rb8_t", [128, 8])
    io8 = sb("io8_t", [128, 8])
    ones1 = sb("ones1", [1, 128])
    comb = sb("comb", [128, 128])
    junk = sb("junk", [128, 1024], BF16)
    az0 = sb("az0", [128, 64])
    az1 = sb("az1", [128, 64])
    rl = sb("rl", [128, 1])
    scol = [sb("scol0", [128, 1]), sb("scol1", [128, 1])]
    s2col = [sb("s2col0", [128, 1]), sb("s2col1", [128, 1])]
    mcol = [sb("mcol0", [128, 1]), sb("mcol1", [128, 1])]
    varc = [sb("varc0", [128, 1]), sb("varc1", [128, 1])]
    mm2 = [sb("mm20", [128, 1]), sb("mm21", [128, 1])]
    rstdc = [sb("rstdc0", [128, 1]), sb("rstdc1", [128, 1])]
    negmr = [sb("negmr0", [128, 1]), sb("negmr1", [128, 1])]
    r8 = [sb("r8_0", [8, 128]), sb("r8_1", [8, 128])]
    raw8 = [sb("raw8_0", [128, 8]), sb("raw8_1", [128, 8])]
    sc8 = [sb("sc8_0", [128, 8]), sb("sc8_1", [128, 8])]
    bal8 = [sb("bal8_0", [128, 8]), sb("bal8_1", [128, 8])]
    bal8b = [sb("bal8b_0", [128, 8]), sb("bal8b_1", [128, 8])]
    bal2 = [sb("bal2_0", [128, 8]), sb("bal2_1", [128, 8])]
    eq8 = [sb("eq8_0", [128, 8]), sb("eq8_1", [128, 8])]
    mi8 = [sb("mi8_0", [128, 8]), sb("mi8_1", [128, 8])]
    oh1 = [sb("oh1_0", [128, 8]), sb("oh1_1", [128, 8])]
    oh2 = [sb("oh2_0", [128, 8]), sb("oh2_1", [128, 8])]
    tmp8 = [sb("tmp8_0", [128, 8]), sb("tmp8_1", [128, 8])]
    m1c = [sb("m1c_0", [128, 1]), sb("m1c_1", [128, 1])]
    i1c = [sb("i1c_0", [128, 1]), sb("i1c_1", [128, 1])]
    g1c = [sb("g1c_0", [128, 1]), sb("g1c_1", [128, 1])]
    g2c = [sb("g2c_0", [128, 1]), sb("g2c_1", [128, 1])]
    d12 = [sb("d12_0", [128, 1]), sb("d12_1", [128, 1])]
    w1c = [sb("w1c_0", [128, 1]), sb("w1c_1", [128, 1])]
    w2c = [sb("w2c_0", [128, 1]), sb("w2c_1", [128, 1])]

    lnbs = [lnb, lnb1]

    pars = nc.alloc_psum_tensor("pars", [128, 4096], F32)
    pq = [pars[:, 0:512], pars[:, 512:1024]]
    prot = [pars[:, 1024:1536], pars[:, 1536:2048]]
    ptb = pars[:, 2048:2560].bitcast(BF16)
    pvB = [pars[:, i * 1024:(i + 1) * 1024] for i in range(4)]
    ptbB = pars[:, 0:512].bitcast(BF16)
    pS = [pars[:, 0:1024], pars[:, 1024:2048], pars[:, 3072:4096]]
    pAV = pars[:, 2048:3072]
    slotC = [pars[:, 3072:3137], pars[:, 3584:3649]]
    pT = [pars[:, 0:1024], pars[:, 1536:2560]]
    pR = [pars[:, 1024:1536], pars[:, 2560:3072]]
    ph = [pars[:, 0:1024], pars[:, 1024:2048]]
    pp = [pars[:, 2048:3072], pars[:, 3072:4096]]

    pr = Prog()
    for s, st in [("t", 1), ("v", 1), ("s", 1), ("g", 1), ("ds", 16),
                  ("do", 16), ("d_xt", 16), ("d_azt", 16), ("d_xqt", 16),
                  ("d_yt", 16), ("d_acc", 16), ("s_az0", 16), ("s_az1", 16),
                  ("s_y", 16), ("s_lnf", 16), ("d_az2", 16), ("d_xq2", 16), ("d_cos", 16), ("d_sin", 16), ("s_y2", 16), ("s_lnf2", 16),
                  ("g_a0", 16), ("g_a1", 16), ("g_a2", 16), ("g_a3", 16),
                  ("g_b0", 16), ("g_b1", 16),
                  ("g_f0", 16), ("g_f1", 16), ("g_f2", 16), ("g_f3", 16),
                  ("g_p0", 16), ("g_p1", 16)]:
        pr.sem(s, st)

    B = Buf
    b_xt = B(); b_lnT = B(); b_lnb = B(); b_wb0 = B(); b_wb1 = B()
    b_lnbs = [b_lnb, B()]; b_stats2 = [B(), B()]; b_junk = B()
    b_qraw2 = [B(), B()]; b_rt12 = [B(), B()]; b_rt22 = [B(), B()]
    b_Kf = B(); b_Qf = B(); b_Vau = B(); b_cos = B(); b_sin = B()
    b_consts = B()
    b_pq = [B(), B()]; b_prot = [B(), B()]; b_ptb = B()
    b_pv = [B(), B(), B(), B()]
    b_pS = [B(), B(), B()]; b_pAV = B(); b_Qz = [B(), B()]
    b_slotC = [b_pS[2], b_pS[2]]
    b_pT = B(); b_pR = B()
    b_ph = [B(), B()]; b_pp = [B(), B()]
    b_stats = B()
    b_atf = B(); b_az = [B(), B()]; b_rl = B()
    b_comb = B(); b_combrow = B()
    b_attnZ = B(); b_yscr = B(); b_lnfs = B()
    b_acc = B(); b_ln3f = B(); b_hf = B()
    b_wfcb = [B(), B(), B(), B()]; b_wpb = [B(), B()]; b_cbc = B(); b_tmpw = B()
    b_wbA = [B(), B(), B(), B()]
    b_out = B()
    b_r8 = B(); b_route = B()

    wb = [wb0, wb1]
    b_wb = [b_wb0, b_wb1]

    b_pb = [B(), B()]
    b_azt = B(); b_xqt = B(); b_yt = B(); b_tmpf = B()
    b_ytile = B(); b_ot = B(); b_yt2 = B()

    pr.do("vector", lambda e: e.memset(ones1[:], 1.0), writes=[b_consts])
    pr.do("sync", lambda e: e.dma_start(idf[:], idf_h), writes=[b_consts], sem="ds")
    pr.do("sync", lambda e: e.dma_start(rtm[:], rtm_h), writes=[b_consts], sem="ds")
    pr.do("sync", lambda e: e.dma_start(idbf[:], idbf_h), writes=[b_consts], sem="ds")
    pr.do("sync", lambda e: e.dma_start(
        centT[:].rearrange("p (c e) -> p c e", c=8),
        centT_h.rearrange("(c p) e -> p c e", p=128)), writes=[b_consts], sem="ds")
    pr.do("sync", lambda e: e.dma_start(rb8[:], rb8_h), writes=[b_consts], sem="ds")
    pr.do("sync", lambda e: e.dma_start(io8[:], io8_h), writes=[b_consts], sem="ds")
    pr.do("sync", lambda e: e.dma_start(cosb[:, 0:NQ], cosq_h), writes=[b_cos], sem="ds")
    pr.do("sync", lambda e: e.dma_start(sinb[:, 0:NQ], sinq_h), writes=[b_sin], sem="ds")
    b_consts.w = ("ds", pr.cnt["ds"])
    b_cos.w = ("ds", pr.cnt["ds"])
    b_sin.w = ("ds", pr.cnt["ds"])

    def ln_stats(src_ap, src_buf, par=0):
        bs = b_stats2[par]
        pr.do("scalar", lambda e: e.activation(
            junk[:], src_ap, AF.Square, accum_out=s2col[par][:]),
            reads=[src_buf], writes=[bs, b_junk])
        pr.do("vector", lambda e: e.reduce_sum(scol[par][:], src_ap, axis=AX.X),
              reads=[src_buf], writes=[bs])

    def ln_finish(src_ap, src_buf, out_ap, out_buf, par=0):
        bs = b_stats2[par]
        pr.do("vector", lambda e: e.tensor_scalar(
            mcol[par][:], scol[par][:], 1.0 / D, None, ALU.mult),
            reads=[bs], writes=[bs])
        pr.do("vector", lambda e: e.tensor_scalar(
            varc[par][:], s2col[par][:], 1.0 / D, EPS, ALU.mult, ALU.add),
            reads=[bs], writes=[bs])
        pr.do("vector", lambda e: e.tensor_mul(
            mm2[par][:], mcol[par][:], mcol[par][:]), reads=[bs], writes=[bs])
        pr.do("vector", lambda e: e.tensor_sub(
            varc[par][:], varc[par][:], mm2[par][:]), reads=[bs], writes=[bs])
        pr.do("scalar", lambda e: e.activation(varc[par][:], varc[par][:], AF.Ln),
              reads=[bs], writes=[bs])
        pr.do("scalar", lambda e: e.activation(
            rstdc[par][:], varc[par][:], AF.Exp, scale=-0.5),
            reads=[bs], writes=[bs])
        pr.do("vector", lambda e: e.tensor_scalar(
            negmr[par][:], mcol[par][:], rstdc[par][:], -1.0, ALU.mult, ALU.mult),
            reads=[bs], writes=[bs])
        pr.do("scalar", lambda e: e.activation(
            out_ap, src_ap, AF.Identity, bias=negmr[par][:], scale=rstdc[par][:]),
            reads=[src_buf, bs], writes=[out_buf])

    def ln_tile(src_ap, src_buf, out_ap, out_buf, par=0):
        ln_stats(src_ap, src_buf, par)
        ln_finish(src_ap, src_buf, out_ap, out_buf, par)

    def ln_transpose(j, ptb_ap, ptb_buf, par=0):
        jb = lnbs[par]
        fns = []
        for dc in range(8):
            fns.append(lambda e, dc=dc, jb=jb: e.transpose(
                ptb_ap[:, dc * 128:(dc + 1) * 128],
                jb[:, dc * 128:(dc + 1) * 128], idbf[:]))
        pr.group("tensor", fns, reads=[b_lnbs[par], b_consts], writes=[ptb_buf])
        dst = lnT[:].rearrange("p (c t) -> p c t", c=8)[:, :, j * 128:(j + 1) * 128]
        src = ptb_ap.rearrange("p (c t) -> p c t", c=8)
        pr.do("vector", lambda e: e.tensor_copy(dst, src),
              reads=[ptb_buf], writes=[b_lnT])

    def qk_pass(nchunks, x_src, wblk_h, dst_view, dst_buf, span, costab_off):
        for tc in range(nchunks):
            pr.do("sync", lambda e, tc=tc: e.dma_start(
                xt[:].rearrange("p (j d) -> p j d", j=4),
                x_src[tc * 512:(tc + 1) * 512, :].rearrange(
                    "(j p) d -> p j d", p=128)),
                writes=[b_xt], sem="d_xt")
            tiles = [xt[:, j * 1024:(j + 1) * 1024] for j in range(4)]
            stages = []
            for j in range(4):
                stages.append((lambda j=j: ln_stats(tiles[j], b_xt, par=j % 2),
                               lambda j=j: ln_finish(tiles[j], b_xt,
                                                     lnbs[j % 2][:],
                                                     b_lnbs[j % 2], par=j % 2),
                               lambda j=j: ln_transpose(j, ptb, b_ptb,
                                                        par=j % 2)))
            stages[0][0](); stages[1][0](); stages[0][1]()
            stages[2][0](); stages[1][1](); stages[0][2]()
            stages[3][0](); stages[2][1](); stages[1][2]()
            stages[3][1](); stages[2][2](); stages[3][2]()

            def w_dma(oc):
                w = oc % 4
                pr.do("gpsimd", lambda e, oc=oc, w=w: e.dma_start(
                    wbA[w][:].rearrange("p (c j) -> p c j", c=8),
                    wblk_h[oc].rearrange("c p j -> p c j")),
                    writes=[b_wbA[w]], sem="g_a%d" % w)

            def pg(oc):
                w = oc % 4
                fns = []
                for dc in range(8):
                    fns.append(lambda e, w=w, dc=dc, oc=oc: e.matmul(
                        pq[oc % 2][:, 0:512], wbA[w][:, dc * 128:(dc + 1) * 128],
                        lnT[:, dc * 512:(dc + 1) * 512],
                        start=(dc == 0), stop=(dc == 7), skip_group_check=True))
                pr.group("tensor", fns, reads=[b_wbA[w], b_lnT],
                         writes=[b_pq[oc % 2]])

            def rope(oc):
                w = oc % 2
                qr = [qraw, qrawB][w]; r1 = [rt1, rt1B][w]; r2 = rt2
                bqr = b_qraw2[w]; br1 = b_rt12[w]; br2 = b_rt22[0]
                pr.do("scalar", lambda e, w=w, qr=qr: e.activation(
                    qr[:], pq[w][:], AF.Identity), reads=[b_pq[w]], writes=[bqr])
                pr.do("tensor", lambda e, w=w, qr=qr: e.matmul(
                    prot[w][:], rtm[:], qr[:], start=True, stop=True),
                    reads=[bqr, b_consts], writes=[b_prot[w]])
                co = costab_off + tc * 512
                pr.do("vector", lambda e, w=w, co=co, r1=r1: e.tensor_mul(
                    r1[:], prot[w][:], sinb[:, co:co + 512]),
                    reads=[b_prot[w], b_sin], writes=[br1])
                pr.do("gpsimd", lambda e, w=w, co=co, qr=qr, r2=r2: e.tensor_mul(
                    r2[:], qr[:], cosb[:, co:co + 512]),
                    reads=[bqr, b_cos], writes=[br2])
                o = oc * span + tc * 512
                pr.do("vector", lambda e, o=o, r1=r1, r2=r2: e.tensor_add(
                    dst_view[:, o:o + 512], r1[:], r2[:]),
                    reads=[br1, br2], writes=[dst_buf])

            w_dma(0); w_dma(1); w_dma(2)
            pg(0)
            for oc in range(8):
                if oc + 3 < 8:
                    w_dma(oc + 3)
                if oc + 1 < 8:
                    pg(oc + 1)
                rope(oc)

    qk_pass(4, xq_h, wqb_h, Q_f, b_Qf, NQ, 0)
    pr.do("sync", lambda e: e.dma_start(cosb[:], cosk_h), writes=[b_cos], sem="d_cos")
    pr.do("sync", lambda e: e.dma_start(sinb[:], sink_h), writes=[b_sin], sem="d_sin")
    qk_pass(8, xkv_h, wkb_h, K_f, b_Kf, NK, 0)

    pr.fence()

    lnbs[1] = qraw[:].bitcast(BF16)
    b_lnbs[1] = b_qraw2[0]
    pr.do("vector", lambda e: e.memset(V_au[:], 1.0), writes=[b_Vau])
    for tc in range(8):
        pr.do("sync", lambda e, tc=tc: e.dma_start(
            xt[:].rearrange("p (j d) -> p j d", j=4),
            xkv_h[tc * 512:(tc + 1) * 512, :].rearrange("(j p) d -> p j d", p=128)),
            writes=[b_xt], sem="d_xt")
        tiles = [xt[:, j * 1024:(j + 1) * 1024] for j in range(4)]
        stages = []
        for j in range(4):
            stages.append((lambda j=j: ln_stats(tiles[j], b_xt, par=j % 2),
                           lambda j=j: ln_finish(tiles[j], b_xt,
                                                 lnbs[j % 2][:],
                                                 b_lnbs[j % 2], par=j % 2),
                           lambda j=j: ln_transpose(j, ptbB, b_pv[0],
                                                    par=j % 2)))
        stages[0][0](); stages[1][0](); stages[0][1]()
        stages[2][0](); stages[1][1](); stages[0][2]()
        stages[3][0](); stages[2][1](); stages[1][2]()
        stages[3][1](); stages[2][2](); stages[3][2]()
        for dc in range(8):
            w = dc % 2
            pr.do("gpsimd", lambda e, dc=dc, w=w: e.dma_start(
                wb[w][:], wvT_h[dc * 128:(dc + 1) * 128, :]),
                writes=[b_wb[w]], sem="g_b%d" % w)
            for j in range(4):
                fns = []
                for nh in range(2):
                    fns.append(lambda e, j=j, dc=dc, w=w, nh=nh: e.matmul(
                        pvB[j][:, nh * 512:(nh + 1) * 512],
                        lnT[:, dc * 512 + j * 128: dc * 512 + (j + 1) * 128],
                        wb[w][:, nh * 512:(nh + 1) * 512],
                        start=(dc == 0), stop=(dc == 7), skip_group_check=True))
                pr.group("tensor", fns, reads=[b_wb[w], b_lnT], writes=[b_pv[j]])
        for j in range(4):
            kt = tc * 4 + j
            dst = V_au[:, kt * 1056:(kt + 1) * 1056].rearrange(
                "p (h x) -> p h x", h=16)[:, :, 0:64]
            src = pvB[j][:].rearrange("p (h x) -> p h x", h=16)
            pr.do("vector", lambda e, dst=dst, src=src: e.tensor_copy(dst, src),
                  reads=[b_pv[j]], writes=[b_Vau])

    pr.fence()

    pr.do("vector", lambda e: e.memset(Qz[0][:], 0.0), writes=[b_Qz[0]])
    pr.do("vector", lambda e: e.memset(Qz[1][:], 0.0), writes=[b_Qz[1]])

    def c_unit(h, qh2):
        hp = (h % 2) * 64
        ho = h // 2
        par = h % 2
        qbase = ho * NQ + qh2 * 1024

        def qz_copy():
            pr.do("vector", lambda e, hp=hp, qbase=qbase, par=par: e.tensor_copy(
                Qz[par][hp:hp + 64, :], Q_f[hp:hp + 64, qbase:qbase + 1024]),
                reads=[b_Qf], writes=[b_Qz[par]])

        def emit_S(kt):
            w = kt % 3
            fns = []
            for nh in range(2):
                o_ap = pS[w][:, nh * 512:(nh + 1) * 512]
                l_ap = K_f[0:128, ho * NK + kt * 128: ho * NK + (kt + 1) * 128]
                r_ap = Qz[par][:, nh * 512:nh * 512 + 512]
                fns.append(lambda e, o=o_ap, l=l_ap, r=r_ap: e.matmul(
                    o, l, r, start=True, stop=True, skip_group_check=True))
            pr.group("tensor", fns, reads=[b_Kf, b_Qz[par]], writes=[b_pS[w]])

        def emit_X(kt):
            w = kt % 3
            pbuf = pb0 if kt % 2 == 0 else pb1
            pr.do("scalar", lambda e, w=w, pbuf=pbuf: e.activation(
                pbuf[:], pS[w][:], AF.Exp, scale=0.125),
                reads=[b_pS[w]], writes=[b_pb[kt % 2]])

        def emit_AV(kt):
            pbuf = pb0 if kt % 2 == 0 else pb1
            fns = []
            for nh in range(2):
                o_ap = pAV[0:65, nh * 512:(nh + 1) * 512]
                l_ap = V_au[:, kt * 1056 + h * 66: kt * 1056 + h * 66 + 65]
                r_ap = pbuf[:, nh * 512:(nh + 1) * 512]
                fns.append(lambda e, o=o_ap, l=l_ap, r=r_ap, kt=kt: e.matmul(
                    o, l, r, start=(kt == 0), stop=(kt == 31),
                    skip_group_check=True))
            pr.group("tensor", fns, reads=[b_Vau, b_pb[kt % 2]], writes=[b_pAV])

        def finalize():
            ab = qh2 * 1024
            pr.do("vector", lambda e, ab=ab: e.tensor_copy(
                atf[0:65, ab:ab + 1024], pAV[0:65, :]),
                reads=[b_pAV], writes=[b_atf])
            fin = []
            for q8 in range(8):
                qt = qh2 * 8 + q8
                sl_ap = slotC[q8 % 2]
                sl_buf = b_slotC[q8 % 2]

                def fin_one(qt=qt, sl_ap=sl_ap, sl_buf=sl_buf, h=h):
                    pr.do("tensor", lambda e, qt=qt, sl_ap=sl_ap: e.transpose(
                        sl_ap, atf[0:65, qt * 128:(qt + 1) * 128],
                        idf[0:65, 0:65]),
                        reads=[b_atf, b_consts], writes=[sl_buf])
                    pr.do("vector", lambda e, sl_ap=sl_ap: e.reciprocal(
                        rl[:], sl_ap[:, 64:65]), reads=[sl_buf], writes=[b_rl])
                    azb = [az0, az1][qt % 2]
                    pr.do("vector",
                          lambda e, sl_ap=sl_ap, azb=azb: e.tensor_scalar(
                              azb[:], sl_ap[:, 0:64], rl[:], None, ALU.mult),
                          reads=[sl_buf, b_rl], writes=[b_az[qt % 2]])
                    pr.do("sync", lambda e, qt=qt, h=h, azb=azb: e.dma_start(
                        attnZ_h[qt * 128:(qt + 1) * 128,
                                h * 64:(h + 1) * 64], azb[:]),
                        reads=[b_az[qt % 2]], sem="s_az%d" % (qt % 2))
                fin.append(fin_one)
            return fin

        return qz_copy, emit_S, emit_X, emit_AV, finalize

    pending_fin = []
    for h in range(16):
        for qh2 in range(2):
            qz_copy, emit_S, emit_X, emit_AV, finalize = c_unit(h, qh2)
            qz_copy()
            emit_S(0); emit_X(0); emit_S(1); emit_X(1)
            for f in pending_fin:
                f()
            pending_fin = []
            for kt in range(32):
                if kt + 2 < 32:
                    emit_S(kt + 2)
                emit_AV(kt)
                if kt + 2 < 32:
                    emit_X(kt + 2)
            pending_fin = finalize()
    for f in pending_fin:
        f()

    pr.fence()

    AZT = [azt, azt1]; XQT = [xqt, xqt1]; YTT = [yt, yt1]; TMPF = [tmpf, tmpf1]
    b_AZT = [b_azt, B()]; b_XQT = [b_xqt, B()]; b_YTT = [b_yt, B()]
    b_TMPF = [b_tmpf, B()]
    b_pT2 = [b_pT, B()]; b_pR2 = [b_pR, B()]
    b_rt2 = [b_route, B()]; b_r82 = [b_r8, B()]
    for j in range(16):
        p = j % 2
        az_, xq_, yy_, tf_ = AZT[p], XQT[p], YTT[p], TMPF[p]
        baz, bxq, byy, btf = b_AZT[p], b_XQT[p], b_YTT[p], b_TMPF[p]
        bt, brt = b_pT2[p], b_rt2[p]
        pTp, pRp = pT[p], pR[p]
        br8 = b_r82[p]
        pr.do("sync", lambda e, j=j, az_=az_: e.dma_start(
            az_, attnZ_h[j * 128:(j + 1) * 128, :]), writes=[baz],
            sem="d_azt" if p == 0 else "d_az2")
        pr.do("sync", lambda e, j=j, xq_=xq_: e.dma_start(
            xq_, xq_h[j * 128:(j + 1) * 128, :]), writes=[bxq],
            sem="d_xqt" if p == 0 else "d_xq2")
        ln_tile(az_, baz, tf_, btf, par=p)
        pr.do("vector", lambda e, yy_=yy_, xq_=xq_, tf_=tf_: e.tensor_add(
            yy_, xq_, tf_), reads=[bxq, btf], writes=[byy])
        pr.do("sync", lambda e, j=j, yy_=yy_: e.dma_start(
            yscr_h[j * 128:(j + 1) * 128, :], yy_), reads=[byy],
            sem="s_y" if p == 0 else "s_y2")
        ln_tile(yy_, byy, tf_, btf, par=p)
        fns = []
        for dc in range(8):
            fns.append(lambda e, dc=dc, pTp=pTp, tf_=tf_: e.transpose(
                pTp[:, dc * 128:(dc + 1) * 128],
                tf_[:, dc * 128:(dc + 1) * 128], idf[:]))
        pr.group("tensor", fns, reads=[btf, b_consts], writes=[bt])
        pr.do("vector", lambda e, az_=az_, pTp=pTp: e.tensor_copy(az_, pTp[:]),
              reads=[bt], writes=[baz])
        pr.do("sync", lambda e, j=j, az_=az_: e.dma_start(lnf_h[j], az_),
              reads=[baz], sem="s_lnf" if p == 0 else "s_lnf2")
        dst = ln3f[:].rearrange("p (c t) -> p c t", c=8)[:, :, j * 128:(j + 1) * 128]
        srcv = az_.rearrange("p (c t) -> p c t", c=8)
        pr.do("vector", lambda e, dst=dst, srcv=srcv: e.tensor_copy(dst, srcv),
              reads=[baz], writes=[b_ln3f])
        fns = []
        for dc in range(8):
            fns.append(lambda e, dc=dc, pRp=pRp, az_=az_: e.matmul(
                pRp[0:8, 0:128], centT[:, dc * 8:(dc + 1) * 8],
                az_[:, dc * 128:(dc + 1) * 128],
                start=(dc == 0), stop=(dc == 7), skip_group_check=True))
        pr.group("tensor", fns, reads=[baz, b_consts], writes=[b_pR2[p]])
        pr.do("vector", lambda e, p=p, pRp=pRp: e.tensor_copy(
            r8[p][:], pRp[0:8, 0:128]), reads=[b_pR2[p]], writes=[br8])
        pr.do("tensor", lambda e, p=p, pRp=pRp: e.transpose(
            pRp[:, 128:136], r8[p][:], idf[0:8, 0:8]),
            reads=[br8, b_consts], writes=[b_pR2[p]])
        pr.do("vector", lambda e, p=p, pRp=pRp: e.tensor_copy(
            raw8[p][:], pRp[:, 128:136]), reads=[b_pR2[p]], writes=[brt])
        pr.do("scalar", lambda e, p=p: e.activation(
            sc8[p][:], raw8[p][:], AF.Sigmoid), reads=[brt], writes=[brt])
        pr.do("vector", lambda e, p=p: e.tensor_add(
            bal8[p][:], raw8[p][:], rb8[:]), reads=[brt, b_consts], writes=[brt])
        pr.do("scalar", lambda e, p=p: e.activation(
            bal8b[p][:], bal8[p][:], AF.Sigmoid), reads=[brt], writes=[brt])
        pr.do("vector", lambda e, p=p: e.reduce_max(
            m1c[p][:], bal8b[p][:], axis=AX.X), reads=[brt], writes=[brt])
        pr.do("vector", lambda e, p=p: e.tensor_single_scalar(
            eq8[p][:], bal8b[p][:], m1c[p][:], ALU.is_ge),
            reads=[brt], writes=[brt])
        pr.do("vector", lambda e, p=p: e.scalar_tensor_tensor(
            mi8[p][:], eq8[p][:], -1024.0, io8[:], ALU.mult, ALU.add),
            reads=[brt, b_consts], writes=[brt])
        pr.do("vector", lambda e, p=p: e.tensor_reduce(
            i1c[p][:], mi8[p][:], axis=AX.X, op=ALU.min),
            reads=[brt], writes=[brt])
        pr.do("vector", lambda e, p=p: e.tensor_scalar(
            i1c[p][:], i1c[p][:], 1024.0, None, ALU.add),
            reads=[brt], writes=[brt])
        pr.do("vector", lambda e, p=p: e.tensor_single_scalar(
            oh1[p][:], io8[:], i1c[p][:], ALU.is_equal),
            reads=[brt, b_consts], writes=[brt])
        pr.do("vector", lambda e, p=p: e.tensor_mul(
            tmp8[p][:], oh1[p][:], sc8[p][:]), reads=[brt], writes=[brt])
        pr.do("vector", lambda e, p=p: e.reduce_sum(
            g1c[p][:], tmp8[p][:], axis=AX.X), reads=[brt], writes=[brt])
        pr.do("vector", lambda e, p=p: e.scalar_tensor_tensor(
            bal2[p][:], oh1[p][:], -1024.0, bal8b[p][:], ALU.mult, ALU.add),
            reads=[brt], writes=[brt])
        pr.do("vector", lambda e, p=p: e.reduce_max(
            m1c[p][:], bal2[p][:], axis=AX.X), reads=[brt], writes=[brt])
        pr.do("vector", lambda e, p=p: e.tensor_single_scalar(
            eq8[p][:], bal2[p][:], m1c[p][:], ALU.is_ge),
            reads=[brt], writes=[brt])
        pr.do("vector", lambda e, p=p: e.scalar_tensor_tensor(
            mi8[p][:], eq8[p][:], -1024.0, io8[:], ALU.mult, ALU.add),
            reads=[brt, b_consts], writes=[brt])
        pr.do("vector", lambda e, p=p: e.tensor_reduce(
            i1c[p][:], mi8[p][:], axis=AX.X, op=ALU.min),
            reads=[brt], writes=[brt])
        pr.do("vector", lambda e, p=p: e.tensor_scalar(
            i1c[p][:], i1c[p][:], 1024.0, None, ALU.add),
            reads=[brt], writes=[brt])
        pr.do("vector", lambda e, p=p: e.tensor_single_scalar(
            oh2[p][:], io8[:], i1c[p][:], ALU.is_equal),
            reads=[brt, b_consts], writes=[brt])
        pr.do("vector", lambda e, p=p: e.tensor_mul(
            tmp8[p][:], oh2[p][:], sc8[p][:]), reads=[brt], writes=[brt])
        pr.do("vector", lambda e, p=p: e.reduce_sum(
            g2c[p][:], tmp8[p][:], axis=AX.X), reads=[brt], writes=[brt])
        pr.do("vector", lambda e, p=p: e.tensor_sub(
            d12[p][:], g2c[p][:], g1c[p][:]), reads=[brt], writes=[brt])
        pr.do("scalar", lambda e, p=p: e.activation(
            w1c[p][:], d12[p][:], AF.Sigmoid, scale=-1.0),
            reads=[brt], writes=[brt])
        pr.do("scalar", lambda e, p=p: e.activation(
            w2c[p][:], d12[p][:], AF.Sigmoid, scale=1.0),
            reads=[brt], writes=[brt])
        pr.do("vector", lambda e, p=p: e.tensor_scalar(
            tmp8[p][:], oh1[p][:], w1c[p][:], None, ALU.mult),
            reads=[brt], writes=[brt])
        pr.do("vector", lambda e, j=j, p=p: e.scalar_tensor_tensor(
            comb[:, j * 8:(j + 1) * 8], oh2[p][:], w2c[p][:], tmp8[p][:],
            ALU.mult, ALU.add), reads=[brt], writes=[b_comb])

    pr.fence()

    for tt in range(2):
        src = lnf_h[tt * 8:(tt + 1) * 8].rearrange("j p (c t) -> p c j t", c=8)
        dst = acc[:].rearrange("p (c j t) -> p c j t", c=8, j=8)
        pr.do("sync", lambda e, dst=dst, src=src: e.dma_start(dst, src),
              writes=[b_acc], sem="d_acc")
        for ei, e_ in enumerate([8, 9, 0, 1, 2, 3, 4, 5, 6, 7]):
            routed = e_ < 8
            if routed:
                fns = []
                for j8 in range(8):
                    jg = tt * 8 + j8
                    o_ap = ph[0][0:1, j8 * 128:(j8 + 1) * 128]
                    i_ap = comb[:, jg * 8 + e_: jg * 8 + e_ + 1]
                    fns.append(lambda e, o=o_ap, i=i_ap: e.transpose(
                        o, i, idf[:]))
                pr.group("tensor", fns, reads=[b_comb, b_consts], writes=[b_ph[0]])
                pr.do("vector", lambda e: e.tensor_copy(
                    combrow_w, ph[0][0:1, 0:1024]),
                    reads=[b_ph[0]], writes=[b_tmpw])
                fns = []
                for j8 in range(8):
                    o_ap = ph[1][:, j8 * 128:(j8 + 1) * 128]
                    r_ap = tmpw[0:1, j8 * 128:(j8 + 1) * 128]
                    fns.append(lambda e, o=o_ap, r=r_ap: e.matmul(
                        o, ones1[:], r, start=True, stop=True,
                        skip_group_check=True))
                pr.group("tensor", fns, reads=[b_tmpw, b_consts], writes=[b_ph[1]])
                pr.do("vector", lambda e: e.tensor_copy(cbc[:], ph[1][:]),
                      reads=[b_ph[1]], writes=[b_cbc])
            for fc in range(32):
                w = fc % 4
                pr.do("gpsimd", lambda e, e_=e_, fc=fc, w=w: e.dma_start(
                    wfcb[w][:].rearrange("p (c j) -> p c j", c=8),
                    wfcT_h[e_, fc].rearrange("c p j -> p c j")),
                    writes=[b_wfcb[w]], sem="g_f%d" % w)
                fns = []
                for dc in range(8):
                    for nh in range(2):
                        fns.append(lambda e, w=w, dc=dc, nh=nh, tt=tt: e.matmul(
                            ph[w % 2][:, nh * 512:(nh + 1) * 512],
                            wfcb[w][:, dc * 128:(dc + 1) * 128],
                            ln3f[:, dc * 2048 + tt * 1024 + nh * 512:
                                 dc * 2048 + tt * 1024 + nh * 512 + 512],
                            start=(dc == 0), stop=(dc == 7), skip_group_check=True))
                pr.group("tensor", fns, reads=[b_wfcb[w], b_ln3f],
                         writes=[b_ph[w % 2]])
                pr.do("scalar", lambda e, fc=fc, w=w: e.activation(
                    h_f[:, fc * 1024:(fc + 1) * 1024], ph[w % 2][:], AF.Gelu),
                    reads=[b_ph[w % 2]], writes=[b_hf])
            for dci in range(8):
                w = dci % 2
                pr.do("gpsimd", lambda e, e_=e_, dci=dci, w=w: e.dma_start(
                    wpb[w][:].rearrange("p (c j) -> p c j", c=32),
                    wprojT_h[e_, dci].rearrange("c p j -> p c j")),
                    writes=[b_wpb[w]], sem="g_p%d" % w)
                fns = []
                for fc in range(32):
                    for nh in range(2):
                        fns.append(lambda e, w=w, fc=fc, nh=nh: e.matmul(
                            pp[w][:, nh * 512:(nh + 1) * 512],
                            wpb[w][:, fc * 128:(fc + 1) * 128],
                            h_f[:, fc * 1024 + nh * 512: fc * 1024 + nh * 512 + 512],
                            start=(fc == 0), stop=(fc == 31), skip_group_check=True))
                pr.group("tensor", fns, reads=[b_wpb[w], b_hf], writes=[b_pp[w]])
                a_sl = acc[:, dci * 1024:(dci + 1) * 1024]
                if routed:
                    pr.do("vector", lambda e, w=w: e.tensor_mul(
                        tmpw[:], pp[w][:], cbc[:]),
                        reads=[b_pp[w], b_cbc], writes=[b_tmpw])
                    pr.do("vector", lambda e, a_sl=a_sl: e.tensor_add(
                        a_sl, a_sl, tmpw[:]), reads=[b_tmpw], writes=[b_acc])
                else:
                    pr.do("vector", lambda e, a_sl=a_sl, w=w: e.tensor_add(
                        a_sl, a_sl, pp[w][:]), reads=[b_pp[w]], writes=[b_acc])
        for j8 in range(8):
            w = j8 % 2
            fns = []
            for dc in range(8):
                fns.append(lambda e, w=w, dc=dc, j8=j8: e.transpose(
                    pp[w][:, dc * 128:(dc + 1) * 128],
                    acc[:, dc * 1024 + j8 * 128: dc * 1024 + (j8 + 1) * 128],
                    idf[:]))
            pr.group("tensor", fns, reads=[b_acc, b_consts], writes=[b_pp[w]])
            t_sq = pr.do("scalar", lambda e, w=w: e.activation(
                junk[:], pp[w][:], AF.Square, accum_out=s2col[0][:]),
                reads=[b_pp[w]], writes=[b_stats, b_junk])
            pr.do("vector", lambda e, w=w: e.reduce_sum(
                scol[0][:], pp[w][:], axis=AX.X),
                reads=[b_pp[w]], writes=[b_stats], after=[t_sq])
            pr.do("vector", lambda e: e.tensor_scalar(
                mcol[0][:], scol[0][:], 1.0 / D, None, ALU.mult),
                reads=[b_stats], writes=[b_stats])
            pr.do("vector", lambda e: e.tensor_scalar(
                varc[0][:], s2col[0][:], 1.0 / D, EPS, ALU.mult, ALU.add),
                reads=[b_stats], writes=[b_stats])
            pr.do("vector", lambda e: e.tensor_mul(
                mm2[0][:], mcol[0][:], mcol[0][:]),
                reads=[b_stats], writes=[b_stats])
            pr.do("vector", lambda e: e.tensor_sub(
                varc[0][:], varc[0][:], mm2[0][:]),
                reads=[b_stats], writes=[b_stats])
            pr.do("scalar", lambda e: e.activation(varc[0][:], varc[0][:], AF.Ln),
                  reads=[b_stats], writes=[b_stats])
            pr.do("scalar", lambda e: e.activation(
                rstdc[0][:], varc[0][:], AF.Exp, scale=-0.5),
                reads=[b_stats], writes=[b_stats])
            pr.do("vector", lambda e: e.tensor_scalar(
                negmr[0][:], mcol[0][:], rstdc[0][:], -1.0, ALU.mult, ALU.mult),
                reads=[b_stats], writes=[b_stats])
            row = tt * 1024 + j8 * 128
            pr.do("sync", lambda e, row=row: e.dma_start(
                ytile, yscr_h[row:row + 128, :]),
                writes=[b_ytile], sem="d_yt")
            pr.do("vector", lambda e: e.tensor_scalar(
                yt2, ytile, negmr[0][:], None, ALU.add),
                reads=[b_ytile, b_stats], writes=[b_yt2])
            pr.do("vector", lambda e, w=w: e.scalar_tensor_tensor(
                ot, pp[w][:], rstdc[0][:], yt2, ALU.mult, ALU.add),
                reads=[b_pp[w], b_yt2, b_stats], writes=[b_ot])
            pr.do("sync", lambda e, row=row: e.dma_start(
                out_h[row:row + 128, :], ot), reads=[b_ot], sem="do")

    pr.fence()

    import contextlib
    with contextlib.ExitStack() as st:
        handles = {}
        for s in pr.cnt:
            handles[s] = st.enter_context(nc.semaphore(s))
        pr.finalize(nc, handles)
    return nc


def _rope_tables():
    pos = np.arange(NK, dtype=np.float32)
    inv = np.power(np.float32(10000.0),
                   np.float32(-2.0) * np.arange(HD // 2, dtype=np.float32)
                   / np.float32(HD))
    ang = pos[:, None] * inv[None, :]
    cos = np.repeat(np.cos(ang), 2, axis=-1)
    sin = np.repeat(np.sin(ang), 2, axis=-1)
    cosT = np.ascontiguousarray(np.vstack([cos.T, cos.T]).astype(np.float32))
    sinT = np.ascontiguousarray(np.vstack([sin.T, sin.T]).astype(np.float32))
    return cosT, sinT


def _rot_matrix():
    r64 = np.zeros((64, 64), np.float32)
    for i in range(32):
        r64[2 * i, 2 * i + 1] = -1.0
        r64[2 * i + 1, 2 * i] = 1.0
    r = np.zeros((128, 128), np.float32)
    r[0:64, 0:64] = r64
    r[64:128, 64:128] = r64
    return np.ascontiguousarray(r.T)


def kernel(L0_attn_x, L0_attn_Wq, L0_attn_Wk, L0_attn_Wv, L0_ffn_centroids,
           L0_ffn_routing_bias, L0_ffn_Wfc_r, L0_ffn_Wproj_r, L0_ffn_Wfc_sh,
           L0_ffn_Wproj_sh):
    x = np.ascontiguousarray(np.asarray(L0_attn_x, dtype=np.float32))
    Wq = np.asarray(L0_attn_Wq, dtype=np.float32)
    Wk = np.asarray(L0_attn_Wk, dtype=np.float32)
    Wv = np.asarray(L0_attn_Wv, dtype=np.float32)
    cent = np.asarray(L0_ffn_centroids, dtype=np.float32)
    rbias = np.asarray(L0_ffn_routing_bias, dtype=np.float32)
    WfcAll = np.concatenate([np.asarray(L0_ffn_Wfc_r, np.float32),
                             np.asarray(L0_ffn_Wfc_sh, np.float32)], axis=0)
    WprojAll = np.concatenate([np.asarray(L0_ffn_Wproj_r, np.float32),
                               np.asarray(L0_ffn_Wproj_sh, np.float32)], axis=0)

    wqb = np.ascontiguousarray(
        Wq.T.reshape(8, 128, 8, 128).transpose(2, 0, 1, 3))
    wkb = np.ascontiguousarray(
        Wk.T.reshape(8, 128, 8, 128).transpose(2, 0, 1, 3))
    wvT = np.ascontiguousarray(Wv.T)
    centT = np.ascontiguousarray(cent.T)
    rb8 = np.ascontiguousarray(np.tile(rbias[None, :], (128, 1)).astype(np.float32))
    io8 = np.ascontiguousarray(
        np.tile(np.arange(8, dtype=np.float32)[None, :], (128, 1)))
    rtm = _rot_matrix()
    idf = np.eye(128, dtype=np.float32)
    idbf = np.eye(128, dtype=ml_dtypes.bfloat16)
    cosT, sinT = _rope_tables()
    wfcT = np.ascontiguousarray(
        WfcAll.reshape(NE, 32, 128, 8, 128).transpose(0, 1, 3, 4, 2))
    wprojT = np.ascontiguousarray(
        WprojAll.reshape(NE, 8, 128, 32, 128).transpose(0, 1, 3, 4, 2))

    nc = build_nc()
    in_maps = []
    for c in range(8):
        b, hf = c // 2, c % 2
        in_maps.append({
            "xq": np.ascontiguousarray(x[b, hf * NQ:(hf + 1) * NQ]),
            "xkv": x[b],
            "wqb": wqb, "wkb": wkb, "wvT": wvT,
            "centT": centT, "rb8": rb8, "io8": io8,
            "rtm": rtm, "idf": idf, "idbf": idbf,
            "cosq": np.ascontiguousarray(cosT[:, hf * NQ:(hf + 1) * NQ]),
            "sinq": np.ascontiguousarray(sinT[:, hf * NQ:(hf + 1) * NQ]),
            "cosk": cosT, "sink": sinT,
            "wfcT": wfcT, "wprojT": wprojT,
        })
    res = run_bass_kernel_spmd(nc, in_maps, core_ids=list(range(8)))
    out = np.empty((4, NK, D), dtype=np.float32)
    for c in range(8):
        b, hf = c // 2, c % 2
        out[b, hf * NQ:(hf + 1) * NQ] = np.asarray(res.results[c]["out"])
    return out
